# version 01
# baseline (speedup 1.0000x reference)
import math
import numpy as np
from contextlib import ExitStack

import concourse.bass as bass
import concourse.mybir as mybir
from concourse.bass_utils import run_bass_kernel_spmd

F32 = mybir.dt.float32
BF16 = mybir.dt.bfloat16
AF = mybir.ActivationFunctionType
ALU = mybir.AluOpType
AXX = mybir.AxisListType.X

B = 2
TSEQ = 8192
D = 2048
TL = 2048
NKC = D // 128
NCORE = 8
HD = 128
NQH = 4
EPS_LN = 1e-5
EPS_RMS = 1e-6
INVSQ = 1.0 / math.sqrt(128.0)

E = 8
H0, HS0 = 1408, 2816
H1, HS1 = 1024, 2048
NHC0, NSC0 = H0 // 128, HS0 // 128
NHC1, NSC1 = H1 // 128, HS1 // 128
WINDOW = 4096


def lhsT_pack(W):
    K, M = W.shape
    assert K % 128 == 0 and M % 128 == 0
    return np.ascontiguousarray(
        W.reshape(K // 128, 128, M // 128, 128).transpose(2, 1, 0, 3))


def pack_small_m(W):
    K, M = W.shape
    assert K % 128 == 0
    return np.ascontiguousarray(W.reshape(K // 128, 128, M).transpose(1, 0, 2))


def rope_tables(theta, swap=False):
    inv = theta ** (-np.arange(64, dtype=np.float64) / 64.0)
    f = np.arange(TSEQ, dtype=np.float64)[:, None] * inv[None, :]
    cs = np.empty((128, TSEQ), dtype=np.float32)
    a, b = (np.sin(f), np.cos(f)) if swap else (np.cos(f), np.sin(f))
    cs[0:64] = a.T.astype(np.float32)
    cs[64:128] = b.T.astype(np.float32)
    return cs


def build_masks():
    i = np.arange(128)[:, None]
    j = np.arange(512)[None, :]
    out = np.zeros((8, 128, 512), dtype=np.float32)
    for n, o in enumerate((0, 128, 256, 384)):
        out[n] = (j >= i + o).astype(np.float32)
        out[4 + n] = (j <= i + o).astype(np.float32)
    return np.ascontiguousarray(out.transpose(1, 0, 2).reshape(128, 8 * 512))


FCONSTS = [0.0, EPS_LN, EPS_RMS, 1.0]


def make_consts():
    fc = np.zeros((128, 8), dtype=np.float32)
    for i, v in enumerate(FCONSTS):
        fc[:, i] = v
    return {
        "ident": np.eye(128, dtype=np.float32),
        "ones": np.ones((128, 128), dtype=np.float32),
        "masks": build_masks(),
        "fconst": fc,
        "cs0": rope_tables(100000.0),
        "cs1": rope_tables(10000.0),
        "cs0s": rope_tables(100000.0, swap=True),
        "cs1s": rope_tables(10000.0, swap=True),
    }



class P:

    def __init__(self, nc):
        self.nc = nc
        self.stack = ExitStack()
        self.block = self.stack.enter_context(nc.Block())
        self.sems = {}
        self.cnt = {}
        for name in ("bar", "dA", "dB", "dC", "dD", "mm", "ms", "ev", "ac",
                     "gp", "cc", "ch"):
            self.sems[name] = self.stack.enter_context(nc.semaphore(name))
            self.cnt[name] = 0

    def inc(self, inst, name, k=1):
        inst.then_inc(self.sems[name], k)
        self.cnt[name] += k
        return inst

    def wait(self, eng, name, val):
        eng.wait_ge(self.sems[name], val)

    def base(self):
        return dict(self.cnt)

    def uid(self):
        self.uids = getattr(self, "uids", 0) + 1
        return self.uids

    def sdma(self, eng, sem, out, in_):
        self.wait(eng, sem, self.cnt[sem])
        return self.inc(eng.dma_start(out=out, in_=in_), sem, 16)

    def bar(self, eng):
        eng.sem_inc(self.sems["bar"], 1)
        self.cnt["bar"] += 1

    def close(self):
        self.stack.close()


class Consts:
    pass


def emit_consts(p, cst, aps):
    nc = p.nc
    cst.ident_f = p.stack.enter_context(nc.sbuf_tensor("c_idf", [128, 128], F32))
    cst.ident_b = p.stack.enter_context(nc.sbuf_tensor("c_idb", [128, 128], BF16))
    cst.ones_f = p.stack.enter_context(nc.sbuf_tensor("c_onf", [128, 128], F32))
    cst.ones_b = p.stack.enter_context(nc.sbuf_tensor("c_onb", [128, 128], BF16))
    cst.masks = p.stack.enter_context(nc.sbuf_tensor("c_msk", [128, 8 * 512], BF16))
    cst.fconst = p.stack.enter_context(nc.sbuf_tensor("c_fc", [128, 8], F32))
    for i, v in enumerate(FCONSTS):
        nc.const_aps.register(F32, v, cst.fconst[:, i:i + 1])

    @p.block.sync
    def _(s):
        p.inc(s.dma_start(out=cst.ident_f[:, :], in_=aps["ident"][:, :]), "dA", 16)
        p.inc(s.dma_start(out=cst.ones_f[:, :], in_=aps["ones"][:, :]), "dA", 16)
        p.inc(s.dma_start(out=cst.fconst[:, :], in_=aps["fconst"][:, :]), "dA", 16)

    @p.block.gpsimd
    def _(g):
        p.inc(g.dma_start(out=cst.ident_b[:, :], in_=aps["ident"][:, :]), "dC", 16)
        p.inc(g.dma_start(out=cst.ones_b[:, :], in_=aps["ones"][:, :]), "dC", 16)
        p.inc(g.dma_start(out=cst.masks[:, :], in_=aps["masks"][:, :]), "dC", 16)
        p.wait(g, "dA", p.cnt["dA"])
        p.wait(g, "dC", p.cnt["dC"])
        p.bar(g)


def mask_ap(cst, n):
    return cst.masks[:, n * 512:(n + 1) * 512]



def emit_load_x(p, x_ap, acc, cst):
    nc, bk = p.nc, p.block
    b0 = p.base()
    with (
        nc.sbuf_tensor("lx_stage", [128, 2 * 2048], F32) as xstage,
        nc.psum_tensor("lx_ps", [128, 2 * 512], F32) as ptr_raw,
    ):
        ptr = ptr_raw
        @bk.sync
        def _(s):
            p.wait(s, "bar", b0["bar"])
            for tt in range(16):
                if tt >= 2:
                    p.wait(s, "mm", b0["mm"] + 16 * (tt - 1))
                p.sdma(s, "dA", xstage[:, (tt % 2) * 2048:(tt % 2) * 2048 + 2048],
                       x_ap[tt * 128:(tt + 1) * 128, :])

        @bk.tensor
        def _(t):
            p.wait(t, "bar", b0["bar"])
            last = None
            for tt in range(16):
                p.wait(t, "dA", b0["dA"] + 16 * (tt + 1))
                for kc in range(16):
                    idx = tt * 16 + kc
                    if idx >= 2:
                        p.wait(t, "ac", b0["ac"] + idx - 1)
                    last = t.transpose(
                        ptr[:, (idx % 2) * 512:(idx % 2) * 512 + 128],
                        xstage[:, (tt % 2) * 2048 + kc * 128:(tt % 2) * 2048 + kc * 128 + 128],
                        cst.ident_f[:, :])
                    p.inc(last, "mm", 1)

        @bk.scalar
        def _(sc):
            p.wait(sc, "bar", b0["bar"])
            last = None
            for tt in range(16):
                for kc in range(16):
                    idx = tt * 16 + kc
                    p.wait(sc, "mm", b0["mm"] + idx + 1)
                    last = sc.copy(acc[:, kc * 2048 + tt * 128: kc * 2048 + tt * 128 + 128],
                                   ptr[:, (idx % 2) * 512:(idx % 2) * 512 + 128])
                    p.inc(last, "ac", 1)
            p.bar(sc)


def emit_load_acc(p, xt_ap, acc):
    b0 = p.base()

    @p.block.sync
    def _(s):
        p.wait(s, "bar", b0["bar"])
        p.inc(s.dma_start(out=acc[:, :].rearrange("p (kc t) -> p kc t", t=TL),
                          in_=xt_ap.rearrange("kc p t -> p kc t")), "dA", 16)
        p.wait(s, "dA", p.cnt["dA"])
        p.bar(s)


def emit_store_acc(p, acc, xt_ap):
    b0 = p.base()

    @p.block.sync
    def _(s):
        p.wait(s, "bar", b0["bar"])
        p.inc(s.dma_start(out=xt_ap.rearrange("kc p t -> p kc t"),
                          in_=acc[:, :].rearrange("p (kc t) -> p kc t", t=TL)), "dB", 16)
        p.wait(s, "dB", p.cnt["dB"])
        p.bar(s)


def emit_ln(p, acc, xn_ap, cst):
    nc, bk = p.nc, p.block
    b0 = p.base()
    with (
        nc.sbuf_tensor("ln_sq", [128, 2 * 512], BF16) as sqst,
        nc.sbuf_tensor("ln_rows", [1, 8 * 512], F32) as rows,
        nc.sbuf_tensor("ln_xt", [128, 512], F32) as xtmp,
        nc.sbuf_tensor("ln_st", [128, 2 * 16 * 512], BF16) as xnst,
        nc.psum_tensor("ln_pss", [1, 512], F32) as ps_s,
        nc.psum_tensor("ln_psq", [1, 512], F32) as ps_q,
        nc.psum_tensor("ln_psa", [128, 512], F32) as ps_a,
        nc.psum_tensor("ln_psb", [128, 512], F32) as ps_b,
    ):
        m_r = rows[0:1, 0:512]
        q_r = rows[0:1, 512:1024]
        v1_r = rows[0:1, 1024:1536]
        v2_r = rows[0:1, 1536:2048]
        a_r = rows[0:1, 2048:2560]
        b_r = rows[0:1, 2560:3072]
        AC, MM, EV = 19, 34, 19

        @bk.scalar
        def _(sc):
            p.wait(sc, "bar", b0["bar"])
            last = None
            for tg in range(4):
                sl = slice(tg * 512, tg * 512 + 512)
                for kc in range(16):
                    if tg * 16 + kc >= 2:
                        p.wait(sc, "mm", b0["mm"] + tg * MM + max(0, 2 * (kc - 1)))
                    p.inc(sc.activation(sqst[:, (kc % 2) * 512:(kc % 2) * 512 + 512],
                                        acc[:, kc * 2048 + tg * 512: kc * 2048 + tg * 512 + 512],
                                        AF.Square), "ac", 1)
                p.wait(sc, "mm", b0["mm"] + tg * MM + 32)
                p.inc(sc.activation(m_r, ps_s[:, :], AF.Copy, scale=np.float32(1.0 / 2048.0)), "ac", 1)
                p.inc(sc.activation(q_r, ps_q[:, :], AF.Copy, scale=np.float32(1.0 / 2048.0)), "ac", 1)
                p.wait(sc, "ev", b0["ev"] + tg * EV + 2)
                sc.activation(v2_r, v2_r, AF.Ln, bias=EPS_LN)
                last = sc.activation(a_r, v2_r, AF.Exp, scale=np.float32(-0.5))
                p.inc(last, "ac", 1)
            p.bar(sc)

        @bk.tensor
        def _(t):
            p.wait(t, "bar", b0["bar"])
            last = None
            for tg in range(4):
                sl = slice(tg * 512, tg * 512 + 512)
                for kc in range(16):
                    if kc == 0 and tg > 0:
                        p.wait(t, "ac", b0["ac"] + (tg - 1) * AC + 18)
                    p.inc(t.matmul(ps_s[:, :], cst.ones_b[:, 0:1],
                                   acc[:, kc * 2048 + tg * 512: kc * 2048 + tg * 512 + 512],
                                   start=(kc == 0), stop=(kc == 15)), "mm", 1)
                    p.wait(t, "ac", b0["ac"] + tg * AC + kc + 1)
                    p.inc(t.matmul(ps_q[:, :], cst.ones_b[:, 0:1],
                                   sqst[:, (kc % 2) * 512:(kc % 2) * 512 + 512],
                                   start=(kc == 0), stop=(kc == 15)), "mm", 1)
                p.wait(t, "ev", b0["ev"] + tg * EV + 3)
                p.inc(t.matmul(ps_a[:, :], cst.ones_f[0:1, :], a_r,
                               start=True, stop=True), "mm", 1)
                last = t.matmul(ps_b[:, :], cst.ones_f[0:1, :], b_r,
                                start=True, stop=True)
                p.inc(last, "mm", 1)

        @bk.vector
        def _(v):
            p.wait(v, "bar", b0["bar"])
            last = None
            for tg in range(4):
                p.wait(v, "ac", b0["ac"] + tg * AC + 18)
                p.inc(v.tensor_mul(v1_r, m_r, m_r), "ev", 1)
                p.inc(v.tensor_sub(v2_r, q_r, v1_r), "ev", 1)
                p.wait(v, "ac", b0["ac"] + tg * AC + 19)
                p.inc(v.scalar_tensor_tensor(b_r, m_r, -1.0, a_r,
                                             ALU.mult, ALU.mult), "ev", 1)
                p.wait(v, "mm", b0["mm"] + tg * MM + 34)
                if tg >= 2:
                    p.wait(v, "dB", b0["dB"] + 16 * (tg - 1))
                st = (tg % 2) * 16 * 512
                for kc in range(16):
                    v.tensor_mul(xtmp[:, :],
                                 acc[:, kc * 2048 + tg * 512: kc * 2048 + tg * 512 + 512],
                                 ps_a[:, :])
                    last = v.tensor_add(xnst[:, st + kc * 512: st + kc * 512 + 512],
                                        xtmp[:, :], ps_b[:, :])
                    p.inc(last, "ev", 1)
            p.bar(v)

        @bk.sync
        def _(s):
            p.wait(s, "bar", b0["bar"])
            for tg in range(4):
                p.wait(s, "ev", b0["ev"] + tg * EV + EV)
                st = (tg % 2) * 16 * 512
                p.sdma(s, "dB",
                       xn_ap[:, :, tg * 512:tg * 512 + 512].rearrange("kc p t -> p kc t"),
                       xnst[:, st:st + 16 * 512].rearrange("p (kc t) -> p kc t", t=512))
            p.wait(s, "dB", b0["dB"] + 64)
            p.bar(s)


def emit_out(p, acc, y_ap, cst):
    nc, bk = p.nc, p.block
    b0 = p.base()
    with (
        nc.sbuf_tensor("o_stage", [128, 2 * 2048], F32) as ostage,
        nc.psum_tensor("o_ps", [128, 2 * 1024], BF16) as pt,
    ):
        @bk.tensor
        def _(t):
            p.wait(t, "bar", b0["bar"])
            last = None
            for tt in range(16):
                for mc in range(16):
                    idx = tt * 16 + mc
                    if idx >= 2:
                        p.wait(t, "ac", b0["ac"] + idx - 1)
                    last = t.transpose(pt[:, (idx % 2) * 1024:(idx % 2) * 1024 + 128],
                                       acc[:, mc * 2048 + tt * 128: mc * 2048 + tt * 128 + 128],
                                       cst.ident_b[:, :])
                    p.inc(last, "mm", 1)

        @bk.scalar
        def _(sc):
            p.wait(sc, "bar", b0["bar"])
            last = None
            for tt in range(16):
                if tt >= 2:
                    p.wait(sc, "dB", b0["dB"] + 16 * (tt - 1))
                for mc in range(16):
                    idx = tt * 16 + mc
                    p.wait(sc, "mm", b0["mm"] + idx + 1)
                    last = sc.copy(ostage[:, (tt % 2) * 2048 + mc * 128:(tt % 2) * 2048 + mc * 128 + 128],
                                   pt[:, (idx % 2) * 1024:(idx % 2) * 1024 + 128])
                    p.inc(last, "ac", 1)
            p.bar(sc)

        @bk.sync
        def _(s):
            p.wait(s, "bar", b0["bar"])
            for tt in range(16):
                p.wait(s, "ac", b0["ac"] + 16 * (tt + 1))
                p.sdma(s, "dB", y_ap[tt * 128:(tt + 1) * 128, :],
                       ostage[:, (tt % 2) * 2048:(tt % 2) * 2048 + 2048])
            p.wait(s, "dB", b0["dB"] + 16 * 16)
            p.bar(s)



def plan_qg(layer, qg):
    lo = 0 if layer == 0 else max(0, 4 * qg - 32)
    out = []
    for kc in range(lo, 4 * qg + 4):
        mi = None
        o = kc * 128 - qg * 512
        if o >= 0:
            mi = o // 128
        elif layer == 1 and qg >= 8:
            d = kc * 128 - (qg * 512 - 4096)
            if 0 <= d <= 384:
                mi = 4 + d // 128
        out.append((kc, mi))
    return out


class Chain:

    def __init__(self, p, base_ch):
        self.p = p
        self.base = base_ch
        self.ops = []
        self.marks = {}
        self.cursor = {}

    def add(self, eng, fn, waits=()):
        self.ops.append((eng, list(waits), fn))
        return len(self.ops)

    def mark(self, key):
        self.marks[key] = len(self.ops)

    def emit_until(self, name, eng, idx):
        p = self.p
        start = self.cursor.get(name, 0)
        for i in range(start, idx):
            en, waits, fn = self.ops[i]
            if en != name:
                continue
            if i > 0:
                p.wait(eng, "ch", self.base + i)
            for (sem, val) in waits:
                p.wait(eng, sem, val)
            p.inc(fn(eng), "ch", 1)
        self.cursor[name] = max(start, idx)

    def emit_all(self, name, eng):
        self.emit_until(name, eng, len(self.ops))


def emit_attn(p, layer, xn_ap, ao_ap, cst, w_aps, cs_ap, cs2_ap, gam_aps):
    nc, bk = p.nc, p.block
    uu = p.uid()
    with (
        nc.sbuf_tensor(f"at{uu}_q", [128, TSEQ], BF16) as q_sb,
        nc.sbuf_tensor(f"at{uu}_k", [128, TSEQ], BF16) as k_sb,
        nc.sbuf_tensor(f"at{uu}_v", [128, TSEQ], BF16) as v_sb,
        nc.sbuf_tensor(f"at{uu}_vt", [128, TSEQ], BF16) as vt_sb,
        nc.sbuf_tensor(f"at{uu}_gam", [128, 2], F32) as gam_sb,
        nc.sbuf_tensor(f"at{uu}_bnb", [128, 1], F32) as bnb_sb,
    ):
        if layer == 1:
            b0 = p.base()

            @bk.sync
            def _(s):
                p.wait(s, "bar", b0["bar"])
                p.inc(s.dma_start(out=gam_sb[:, 0:1], in_=gam_aps[0][:, :]), "dA", 16)
                p.inc(s.dma_start(out=gam_sb[:, 1:2], in_=gam_aps[1][:, :]), "dA", 16)
                p.wait(s, "dA", p.cnt["dA"])
                p.bar(s)

        if layer == 0:
            groups = [(0, [0, 1, 2, 3])]
        else:
            groups = [(i, [i]) for i in range(4)]
        for kv, qheads in groups:
            for pi, qh in enumerate(qheads):
                chunks = []
                if pi == 0:
                    chunks += [("k", kv, k_sb), ("v", kv, v_sb)]
                chunks.append(("q", qh, q_sb))
                _attn_pass(p, layer, chunks, xn_ap, cs_ap, cs2_ap, w_aps, cst,
                           q_sb, k_sb, v_sb, vt_sb, gam_sb, bnb_sb,
                           build_vt=(pi == 0))
                _attn_scores(p, layer, qh, q_sb, k_sb, vt_sb, bnb_sb, cst,
                             ao_ap)


def _attn_pass(p, layer, chunks, xn_ap, cs_ap, cs2_ap, w_aps, cst,
               q_sb, k_sb, v_sb, vt_sb, gam_sb, bnb_sb, build_vt):
    nc, bk = p.nc, p.block
    b0 = p.base()
    nch = len(chunks)
    u = p.uid()
    with (
        nc.sbuf_tensor(f"a{u}_xn", [128, 2 * 16 * 512], BF16) as xn_sb,
        nc.sbuf_tensor(f"a{u}_cs", [128, 2 * 512], F32) as cs_sb,
        nc.sbuf_tensor(f"a{u}_cs2", [128, 2 * 512], F32) as cs2_sb,
        nc.sbuf_tensor(f"a{u}_w", [128, nch * 2048], BF16) as w_sb,
        nc.sbuf_tensor(f"a{u}_sq", [128, 512], BF16) as sq_sb,
        nc.sbuf_tensor(f"a{u}_ro", [128, 512], F32) as ro_sb,
        nc.sbuf_tensor(f"a{u}_rq", [64, 2 * 512], F32) as rq_sb,
        nc.sbuf_tensor(f"a{u}_rows", [1, 4 * 512], F32) as rows,
        nc.sbuf_tensor(f"a{u}_nsq", [1, 2 * TSEQ], F32) as nsq,
        nc.sbuf_tensor(f"a{u}_sc1", [1, 8], F32) as sc1,
        nc.sbuf_tensor(f"a{u}_qn", [128, 512], F32) as qn_sb,
        nc.psum_tensor(f"a{u}_psp", [128, nch * 512], F32) as ps_proj,
        nc.psum_tensor(f"a{u}_psn", [1, 512], F32) as ps_n,
        nc.psum_tensor(f"a{u}_psr", [128, 512], F32) as ps_r,
        nc.psum_tensor(f"a{u}_pt", [128, 2 * 1024], BF16) as pt,
    ):
        rw = rows[0:1, 0:512]
        rw2 = rows[0:1, 512:1024]
        qraw = ro_sb
        ch = Chain(p, p.cnt["ch"])
        base_ch = p.cnt["ch"]
        proj_mm = {}
        mm_per_gg = nch

        def emit_chain_ops():
            for gg in range(16):
                sl = slice(gg * 512, gg * 512 + 512)
                cs0 = (gg % 2) * 512
                csA = cs_sb[0:64, cs0:cs0 + 512]
                csB = cs_sb[64:128, cs0:cs0 + 512]
                cs2A = cs2_sb[0:64, cs0:cs0 + 512]
                cs2B = cs2_sb[64:128, cs0:cs0 + 512]
                for ci, (kind, hidx, out_sb) in enumerate(chunks):
                    ps = ps_proj[:, ci * 512:(ci + 1) * 512]
                    pw = [("mm", b0["mm"] + gg * mm_per_gg + ci + 1)]
                    if kind == "v":
                        ch.add("sc", lambda sc, ps=ps, sl=sl, o=out_sb:
                               sc.copy(o[:, sl], ps), waits=pw)
                        ch.mark(("psfree", gg, ci))
                        ch.mark(("done", gg, ci))
                        continue
                    if layer == 0:
                        ch.add("v", lambda v, ps=ps, c=csA: v.tensor_mul(
                            rq_sb[:, 0:512], ps[0:64, :], c), waits=pw)
                        ch.add("v", lambda v, ps=ps, c=csB: v.tensor_mul(
                            rq_sb[:, 512:1024], ps[64:128, :], c))
                        ch.add("v", lambda v: v.tensor_add(
                            ro_sb[0:64, :], rq_sb[:, 0:512], rq_sb[:, 512:1024]))
                        ch.add("v", lambda v, ps=ps, c=csA: v.tensor_mul(
                            rq_sb[:, 0:512], ps[64:128, :], c))
                        ch.add("v", lambda v, ps=ps, c=csB: v.tensor_mul(
                            rq_sb[:, 512:1024], ps[0:64, :], c))
                        ch.add("v", lambda v: v.tensor_sub(
                            ro_sb[64:128, :], rq_sb[:, 0:512], rq_sb[:, 512:1024]))
                        ch.mark(("psfree", gg, ci))
                        ch.add("sc", lambda sc: sc.activation(
                            sq_sb[:, :], ro_sb[:, :], AF.Square))
                        ch.add("t", lambda t: t.matmul(
                            ps_n[:, :], cst.ones_b[:, 0:1], sq_sb[:, :],
                            start=True, stop=True))
                        ch.add("sc", lambda sc: sc.activation(
                            rw, ps_n[:, :], AF.Ln, scale=np.float32(1.0 / 128.0),
                            bias=EPS_RMS))
                        ch.add("sc", lambda sc: sc.activation(
                            rw2, rw, AF.Exp, scale=np.float32(-0.5)))
                        ch.add("t", lambda t: t.matmul(
                            ps_r[:, :], cst.ones_f[0:1, :], rw2,
                            start=True, stop=True))
                        ch.add("v", lambda v, sl=sl, o=out_sb: v.tensor_mul(
                            o[:, sl], ro_sb[:, :], ps_r[:, :]))
                        ch.mark(("done", gg, ci))
                    else:
                        gam = gam_sb[:, 0:1] if kind == "q" else gam_sb[:, 1:2]
                        ch.add("sc", lambda sc, ps=ps: sc.activation(
                            sq_sb[:, :], ps, AF.Square), waits=pw)
                        ch.add("t", lambda t: t.matmul(
                            ps_n[:, :], cst.ones_b[:, 0:1], sq_sb[:, :],
                            start=True, stop=True))
                        ch.add("sc", lambda sc, ps=ps: sc.copy(qraw[:, :], ps))
                        ch.mark(("psfree", gg, ci))
                        ch.add("sc", lambda sc: sc.activation(
                            rw, ps_n[:, :], AF.Ln, scale=np.float32(1.0 / 128.0),
                            bias=EPS_RMS))
                        ch.add("sc", lambda sc: sc.activation(
                            rw2, rw, AF.Exp, scale=np.float32(-0.5)))
                        ch.add("t", lambda t: t.matmul(
                            ps_r[:, :], cst.ones_f[0:1, :], rw2,
                            start=True, stop=True))
                        ch.add("v", lambda v, gam=gam: v.scalar_tensor_tensor(
                            qn_sb[:, :], qraw[:, :], gam, ps_r[:, :],
                            ALU.mult, ALU.mult))
                        ch.add("v", lambda v, c=csA: v.tensor_mul(
                            rq_sb[:, 0:512], qn_sb[0:64, :], c))
                        ch.add("v", lambda v, c=csB: v.tensor_mul(
                            rq_sb[:, 512:1024], qn_sb[64:128, :], c))
                        ch.add("v", lambda v, sl=sl, o=out_sb: v.tensor_sub(
                            o[0:64, sl], rq_sb[:, 0:512], rq_sb[:, 512:1024]))
                        ch.add("v", lambda v, c=cs2B: v.tensor_mul(
                            rq_sb[:, 0:512], qn_sb[64:128, :], c))
                        ch.add("v", lambda v, c=cs2A: v.tensor_mul(
                            rq_sb[:, 512:1024], qn_sb[0:64, :], c))
                        ch.add("v", lambda v, sl=sl, o=out_sb: v.tensor_add(
                            o[64:128, sl], rq_sb[:, 0:512], rq_sb[:, 512:1024]))
                        ch.add("sc", lambda sc, sl=sl, o=out_sb: sc.activation(
                            sq_sb[:, :], o[:, sl], AF.Square))
                        ch.add("t", lambda t: t.matmul(
                            ps_n[:, :], cst.ones_b[:, 0:1], sq_sb[:, :],
                            start=True, stop=True))
                        nrow = nsq[0:1, 0 * TSEQ + gg * 512: 0 * TSEQ + gg * 512 + 512] \
                            if kind == "q" else \
                            nsq[0:1, 1 * TSEQ + gg * 512: 1 * TSEQ + gg * 512 + 512]
                        ch.add("sc", lambda sc, nr=nrow: sc.copy(nr, ps_n[:, :]))
                        ch.mark(("done", gg, ci))
            if build_vt:
                for kg in range(64):
                    ch.add("t", lambda t, kg=kg: t.transpose(
                        pt[:, (kg % 2) * 1024:(kg % 2) * 1024 + 128],
                        v_sb[:, kg * 128:kg * 128 + 128], cst.ident_b[:, :]))
                    ch.add("v", lambda v, kg=kg: v.tensor_copy(
                        vt_sb[:, kg * 128:kg * 128 + 128],
                        pt[:, (kg % 2) * 1024:(kg % 2) * 1024 + 128]))
            if layer == 1:
                nq_r = nsq[0:1, 0:TSEQ]
                nk_r = nsq[0:1, TSEQ:2 * TSEQ]
                ch.add("sc", lambda sc: sc.activation(nq_r, nq_r, AF.Ln,
                                                      bias=np.float32(0.0)))
                ch.add("sc", lambda sc: sc.activation(nq_r, nq_r, AF.Exp,
                                                      scale=np.float32(0.5)))
                ch.add("v", lambda v: v.reduce_max(sc1[0:1, 0:1], nq_r, axis=AXX))
                if build_vt:
                    ch.add("sc", lambda sc: sc.activation(nk_r, nk_r, AF.Ln,
                                                          bias=np.float32(0.0)))
                    ch.add("sc", lambda sc: sc.activation(nk_r, nk_r, AF.Exp,
                                                          scale=np.float32(0.5)))
                    ch.add("v", lambda v: v.reduce_max(sc1[0:1, 1:2], nk_r, axis=AXX))
                ch.add("v", lambda v: v.tensor_mul(sc1[0:1, 2:3], sc1[0:1, 0:1],
                                                   sc1[0:1, 1:2]))
                ch.add("v", lambda v: v.tensor_scalar(
                    sc1[0:1, 3:4], sc1[0:1, 2:3], float(-INVSQ),
                    20.0, ALU.mult, ALU.add))
                ch.add("t", lambda t: t.matmul(ps_r[:, 0:1], cst.ones_f[0:1, :],
                                               sc1[0:1, 3:4], start=True, stop=True))
                ch.add("sc", lambda sc: sc.copy(bnb_sb[:, :], ps_r[:, 0:1]))

        emit_chain_ops()

        @bk.gpsimd
        def _(g):
            p.wait(g, "bar", b0["bar"])
            for ci, (kind, hidx, _o) in enumerate(chunks):
                p.inc(g.dma_start(
                    out=w_sb[:, ci * 2048:(ci + 1) * 2048],
                    in_=w_aps[kind][hidx].rearrange("p kc m -> p (kc m)")), "dC", 16)

        @bk.sync
        def _(s):
            p.wait(s, "bar", b0["bar"])
            for gg in range(16):
                if gg >= 2:
                    p.wait(s, "mm", b0["mm"] + (gg - 1) * mm_per_gg)
                    p.wait(s, "ch", base_ch + ch.marks[("done", gg - 2, nch - 1)])
                r, tl = gg // 4, (gg % 4) * 512
                p.sdma(s, "dA",
                       xn_sb[:, (gg % 2) * 16 * 512:((gg % 2) + 1) * 16 * 512]
                           .rearrange("p (kc t) -> p kc t", t=512),
                       xn_ap[r, :, :, tl:tl + 512].rearrange("kc p t -> p kc t"))
                p.sdma(s, "dA", cs_sb[:, (gg % 2) * 512:(gg % 2) * 512 + 512],
                       cs_ap[:, gg * 512:gg * 512 + 512])
                p.sdma(s, "dA", cs2_sb[:, (gg % 2) * 512:(gg % 2) * 512 + 512],
                       cs2_ap[:, gg * 512:gg * 512 + 512])

        @bk.tensor
        def _(t):
            p.wait(t, "bar", b0["bar"])
            p.wait(t, "dC", b0["dC"] + 16 * nch)
            for gg in range(16):
                p.wait(t, "dA", b0["dA"] + 48 * (gg + 1))
                for ci, (kind, hidx, _o) in enumerate(chunks):
                    if gg >= 1:
                        p.wait(t, "ch", base_ch + ch.marks[("psfree", gg - 1, ci)])
                    for kc in range(16):
                        mmi = t.matmul(
                            ps_proj[:, ci * 512:(ci + 1) * 512],
                            w_sb[:, ci * 2048 + kc * 128: ci * 2048 + kc * 128 + 128],
                            xn_sb[:, ((gg % 2) * 16 + kc) * 512:((gg % 2) * 16 + kc) * 512 + 512],
                            start=(kc == 0), stop=(kc == 15))
                    p.inc(mmi, "mm", 1)
                ch.emit_until("t", t, ch.marks[("done", gg, nch - 1)])
            ch.emit_all("t", t)

        @bk.vector
        def _(v):
            p.wait(v, "bar", b0["bar"])
            ch.emit_all("v", v)
            p.bar(v)

        @bk.scalar
        def _(sc):
            p.wait(sc, "bar", b0["bar"])
            ch.emit_all("sc", sc)
            p.bar(sc)

        @bk.gpsimd
        def _(g):
            p.wait(g, "ch", base_ch + len(ch.ops))
            p.bar(g)


def _attn_scores(p, layer, qh, q_sb, k_sb, vt_sb, bnb_sb, cst, ao_ap):
    nc, bk = p.nc, p.block
    b0 = p.base()
    u = p.uid()
    plans = [plan_qg(layer, qg) for qg in range(16)]
    trios = []
    ev_cum = []
    n_ev = 0
    for qg in range(16):
        n = len(plans[qg])
        for j, (kc, mi) in enumerate(plans[qg]):
            g = len(trios)
            if mi is not None:
                n_ev += 1
            trios.append((qg, kc, mi, j == 0, j == n - 1))
            ev_cum.append(n_ev)
        n_ev += 2
    norm_ev = []
    n_ev = 0
    for qg in range(16):
        for j, (kc, mi) in enumerate(plans[qg]):
            if mi is not None:
                n_ev += 1
        n_ev += 2
        norm_ev.append(n_ev)

    if layer == 0:
        exp_bias = np.float32(0.0)
    else:
        exp_bias = bnb_sb[:, 0:1]

    with (
        nc.sbuf_tensor(f"s{u}_p", [128, 2 * 512], BF16) as p_sb,
        nc.sbuf_tensor(f"s{u}_rcp", [128, 512], F32) as rcp_sb,
        nc.sbuf_tensor(f"s{u}_ao", [128, 2 * 512], BF16) as ao_st,
        nc.psum_tensor(f"s{u}_ps", [128, 2 * 512], F32) as ps_s,
        nc.psum_tensor(f"s{u}_po", [128, 512], F32) as ps_o,
        nc.psum_tensor(f"s{u}_pd", [128, 512], F32) as ps_d,
    ):
        @bk.tensor
        def _(t):
            p.wait(t, "bar", b0["bar"])
            pend = None

            def emit_pvden(g):
                qg, kc, mi, first, last = trios[g]
                if mi is not None:
                    p.wait(t, "ev", b0["ev"] + ev_cum[g])
                else:
                    p.wait(t, "ac", b0["ac"] + g + 1)
                if first and qg > 0:
                    p.wait(t, "ev", b0["ev"] + norm_ev[qg - 1])
                sl = slice((g % 2) * 512, (g % 2) * 512 + 512)
                t.matmul(ps_o[:, :], vt_sb[:, kc * 128:kc * 128 + 128],
                         p_sb[:, sl], start=first, stop=last)
                p.inc(t.matmul(ps_d[:, :], cst.ones_b[:, :], p_sb[:, sl],
                               start=first, stop=last), "mm", 1)

            for g, (qg, kc, mi, first, last) in enumerate(trios):
                if g >= 2:
                    p.wait(t, "ac", b0["ac"] + g - 1)
                p.inc(t.matmul(ps_s[:, (g % 2) * 512:(g % 2) * 512 + 512],
                               k_sb[:, kc * 128:kc * 128 + 128],
                               q_sb[:, qg * 512:qg * 512 + 512],
                               start=True, stop=True), "ms", 1)
                if pend is not None:
                    emit_pvden(pend)
                pend = g
            emit_pvden(pend)

        @bk.scalar
        def _(sc):
            p.wait(sc, "bar", b0["bar"])
            for g, (qg, kc, mi, first, last) in enumerate(trios):
                p.wait(sc, "ms", b0["ms"] + g + 1)
                if g >= 2:
                    p.wait(sc, "mm", b0["mm"] + g - 1)
                p.inc(sc.activation(p_sb[:, (g % 2) * 512:(g % 2) * 512 + 512],
                                    ps_s[:, (g % 2) * 512:(g % 2) * 512 + 512],
                                    AF.Exp, bias=exp_bias,
                                    scale=np.float32(INVSQ)), "ac", 1)

        @bk.vector
        def _(v):
            p.wait(v, "bar", b0["bar"])
            g_at = 0
            for qg in range(16):
                nt = len(plans[qg])
                for j, (kc, mi) in enumerate(plans[qg]):
                    g = g_at + j
                    if mi is not None:
                        p.wait(v, "ac", b0["ac"] + g + 1)
                        sl = slice((g % 2) * 512, (g % 2) * 512 + 512)
                        p.inc(v.tensor_mul(p_sb[:, sl], p_sb[:, sl],
                                           mask_ap(cst, mi)), "ev", 1)
                g_at += nt
                p.wait(v, "mm", b0["mm"] + g_at)
                p.inc(v.reciprocal(rcp_sb[:, :], ps_d[:, :]), "ev", 1)
                if qg >= 2:
                    p.wait(v, "dB", b0["dB"] + 16 * (qg - 1))
                p.inc(v.tensor_mul(ao_st[:, (qg % 2) * 512:(qg % 2) * 512 + 512],
                                   ps_o[:, :], rcp_sb[:, :]), "ev", 1)
            p.bar(v)

        @bk.sync
        def _(s):
            p.wait(s, "bar", b0["bar"])
            for qg in range(16):
                p.wait(s, "ev", b0["ev"] + norm_ev[qg])
                p.sdma(s, "dB",
                       ao_ap[qg // 4, qh, :, (qg % 4) * 512:(qg % 4) * 512 + 512],
                       ao_st[:, (qg % 2) * 512:(qg % 2) * 512 + 512])
            p.wait(s, "dB", b0["dB"] + 16 * 16)
            p.bar(s)



def emit_rest(p, layer, xn_ap, ao_ap, acc, cst, wr):
    nc, bk = p.nc, p.block
    nhc = NHC0 if layer == 0 else NHC1
    u = p.uid()

    b0 = p.base()
    with (
        nc.sbuf_tensor(f"r{u}_ao", [128, 16 * 2048], BF16) as ao_sb,
        nc.sbuf_tensor(f"r{u}_wo", [128, 2 * 2048], BF16) as wo_sb,
        nc.psum_tensor(f"r{u}_pw", [128, 2 * 512], F32) as pw,
    ):
        @bk.sync
        def _(s):
            p.wait(s, "bar", b0["bar"])
            p.inc(s.dma_start(
                out=ao_sb[:, :].rearrange("p (h t) -> p h t", t=2048),
                in_=ao_ap.rearrange("h p t -> p h t")), "dA", 16)

        @bk.gpsimd
        def _(g):
            p.wait(g, "bar", b0["bar"])
            for mc in range(16):
                if mc >= 2:
                    p.wait(g, "mm", b0["mm"] + (mc - 1) * 4)
                p.sdma(g, "dC", wo_sb[:, (mc % 2) * 2048:(mc % 2) * 2048 + 2048],
                       wr["wo"][mc].rearrange("p kc m -> p (kc m)"))

        @bk.tensor
        def _(t):
            p.wait(t, "bar", b0["bar"])
            p.wait(t, "dA", b0["dA"] + 16)
            for mc in range(16):
                p.wait(t, "dC", b0["dC"] + 16 * (mc + 1))
                for tg in range(4):
                    idx = mc * 4 + tg
                    if idx >= 2:
                        p.wait(t, "ev", b0["ev"] + idx - 1)
                    for hc in range(16):
                        mi = t.matmul(
                            pw[:, (idx % 2) * 512:(idx % 2) * 512 + 512],
                            wo_sb[:, (mc % 2) * 2048 + hc * 128:(mc % 2) * 2048 + hc * 128 + 128],
                            ao_sb[:, hc * 2048 + tg * 512: hc * 2048 + tg * 512 + 512],
                            start=(hc == 0), stop=(hc == 15))
                    p.inc(mi, "mm", 1)

        @bk.vector
        def _(v):
            p.wait(v, "bar", b0["bar"])
            for mc in range(16):
                for tg in range(4):
                    idx = mc * 4 + tg
                    p.wait(v, "mm", b0["mm"] + idx + 1)
                    a = acc[:, mc * 2048 + tg * 512: mc * 2048 + tg * 512 + 512]
                    p.inc(v.tensor_add(a, a, pw[:, (idx % 2) * 512:(idx % 2) * 512 + 512]),
                          "ev", 1)
            p.bar(v)

    comb_dram = nc.dram_tensor(f"r{u}_combd", [8, 2048], F32).ap()
    with (
        nc.sbuf_tensor(f"r{u}_xn", [128, 16 * 1024], BF16) as xn_sb,
        nc.sbuf_tensor(f"r{u}_comb", [8, 2048], F32) as comb_fm,
        nc.sbuf_tensor(f"r{u}_sgbc", [128, 2048], F32) as sg_bc,
        nc.sbuf_tensor(f"r{u}_cbc", [128, 2048], F32) as c_bc,
    ):
        _emit_router(p, layer, xn_ap, cst, wr, xn_sb, comb_fm,
                     comb_dram, sg_bc, u)
        _emit_experts(p, layer, nhc, xn_ap, acc, cst, wr, xn_sb, comb_dram,
                      sg_bc, c_bc, u)


def _emit_router(p, layer, xn_ap, cst, wr, xn_sb, comb_fm,
                 comb_dram, sg_bc, u):
    nc, bk = p.nc, p.block
    b0 = p.base()
    with (
        nc.sbuf_tensor(f"r{u}_gt", [128, 128], BF16) as gate_sb,
        nc.sbuf_tensor(f"r{u}_sgw", [128, 16], BF16) as sgw_sb,
        nc.sbuf_tensor(f"r{u}_log", [8, 2048], F32) as log_sb,
        nc.sbuf_tensor(f"r{u}_sgr", [1, 2048], F32) as sg_row,
        nc.sbuf_tensor(f"r{u}_lt", [128, 8], F32) as lt_sb,
        nc.sbuf_tensor(f"r{u}_rs", [128, 8], F32) as rs_sb,
        nc.sbuf_tensor(f"r{u}_cmb", [128, 8], F32) as cmb_sb,
        nc.psum_tensor(f"r{u}_pl", [8, 512], F32) as ps_l,
        nc.psum_tensor(f"r{u}_pn", [1, 512], F32) as ps_n,
        nc.psum_tensor(f"r{u}_px", [128, 512], F32) as ps_x,
        nc.psum_tensor(f"r{u}_pt8", [128, 512], F32) as pt8,
    ):
        ch = Chain(p, p.cnt["ch"])
        base_ch = p.cnt["ch"]
        for tg in range(4):
            pw_l = [("mm", b0["mm"] + 2 * tg + 1)]
            pw_n = [("mm", b0["mm"] + 2 * tg + 2)]
            ch.add("v", lambda v, tg=tg: v.tensor_copy(
                log_sb[:, tg * 512:tg * 512 + 512], ps_l[:, :]), waits=pw_l)
            ch.mark(("lfree", tg))
            ch.add("sc", lambda sc, tg=tg: sc.activation(
                sg_row[0:1, tg * 512:tg * 512 + 512], ps_n[:, :], AF.Sigmoid),
                waits=pw_n)
            ch.mark(("nfree", tg))
        for tt in range(16):
            ch.add("t", lambda t, tt=tt: t.transpose(
                pt8[:, 0:8], log_sb[:, tt * 128:tt * 128 + 128],
                cst.ident_f[0:8, 0:8]))
            ch.add("v", lambda v: v.tensor_copy(lt_sb[:, :], pt8[:, 0:8]))
            ch.add("v", lambda v: v.reduce_max(rs_sb[:, 0:1], lt_sb[:, :], axis=AXX))
            ch.add("v", lambda v: v.tensor_scalar(
                cmb_sb[:, :], lt_sb[:, :], rs_sb[:, 0:1], None, ALU.is_ge))
            ch.add("v", lambda v: v.scalar_tensor_tensor(
                lt_sb[:, :], cmb_sb[:, :], -1e30, lt_sb[:, :], ALU.mult, ALU.add))
            ch.add("v", lambda v: v.reduce_max(rs_sb[:, 1:2], lt_sb[:, :], axis=AXX))
            ch.add("v", lambda v: v.tensor_sub(rs_sb[:, 2:3], rs_sb[:, 1:2],
                                               rs_sb[:, 0:1]))
            ch.add("sc", lambda sc: sc.activation(rs_sb[:, 3:4], rs_sb[:, 2:3],
                                                  AF.Exp))
            ch.add("v", lambda v: v.tensor_scalar(rs_sb[:, 4:5], rs_sb[:, 3:4],
                                                  1.0, None, ALU.add))
            ch.add("v", lambda v: v.reciprocal(rs_sb[:, 5:6], rs_sb[:, 4:5]))
            ch.add("v", lambda v: v.tensor_mul(rs_sb[:, 6:7], rs_sb[:, 3:4],
                                               rs_sb[:, 5:6]))
            ch.add("v", lambda v: v.tensor_scalar(
                lt_sb[:, :], lt_sb[:, :], rs_sb[:, 1:2], None, ALU.is_ge))
            ch.add("v", lambda v: v.tensor_scalar(
                cmb_sb[:, :], cmb_sb[:, :], rs_sb[:, 5:6], None, ALU.mult))
            ch.add("v", lambda v: v.scalar_tensor_tensor(
                cmb_sb[:, :], lt_sb[:, :], rs_sb[:, 6:7], cmb_sb[:, :],
                ALU.mult, ALU.add))
            ch.add("t", lambda t: t.transpose(
                pt8[0:8, 0:128], cmb_sb[:, :], cst.ident_f[:, :]))
            ch.add("v", lambda v, tt=tt: v.tensor_copy(
                comb_fm[:, tt * 128:tt * 128 + 128], pt8[0:8, 0:128]))
        for j in range(4):
            ch.add("t", lambda t, j=j: t.matmul(
                ps_x[:, :], cst.ones_f[0:1, :],
                sg_row[0:1, j * 512:j * 512 + 512], start=True, stop=True))
            ch.add("v", lambda v, j=j: v.tensor_copy(
                sg_bc[:, j * 512:j * 512 + 512], ps_x[:, :]))

        @bk.gpsimd
        def _(g):
            p.wait(g, "bar", b0["bar"])
            p.inc(g.dma_start(out=gate_sb[:, :], in_=wr["gate"][:, :]), "dC", 16)
            p.inc(g.dma_start(out=sgw_sb[:, :], in_=wr["sgw"][:, :]), "dC", 16)

        @bk.sync
        def _(s):
            p.wait(s, "bar", b0["bar"])
            for tg in range(4):
                if tg >= 2:
                    p.wait(s, "mm", b0["mm"] + 2 * (tg - 1))
                p.sdma(s, "dA",
                       xn_sb[:, (tg % 2) * 16 * 512:((tg % 2) + 1) * 16 * 512]
                           .rearrange("p (kc t) -> p kc t", t=512),
                       xn_ap[:, :, tg * 512:tg * 512 + 512]
                           .rearrange("kc p t -> p kc t"))
            p.wait(s, "ch", base_ch + len(ch.ops))
            p.inc(s.dma_start(out=comb_dram[:, :], in_=comb_fm[:, :]), "dB", 16)
            p.wait(s, "dB", p.cnt["dB"])
            p.bar(s)

        @bk.tensor
        def _(t):
            p.wait(t, "bar", b0["bar"])
            p.wait(t, "dC", b0["dC"] + 32)
            for tg in range(4):
                p.wait(t, "dA", b0["dA"] + 16 * (tg + 1))
                if tg >= 1:
                    p.wait(t, "ch", base_ch + ch.marks[("lfree", tg - 1)])
                for kc in range(16):
                    mi = t.matmul(ps_l[:, :], gate_sb[:, kc * 8:kc * 8 + 8],
                                  xn_sb[:, ((tg % 2) * 16 + kc) * 512:((tg % 2) * 16 + kc) * 512 + 512],
                                  start=(kc == 0), stop=(kc == 15))
                p.inc(mi, "mm", 1)
                if tg >= 1:
                    p.wait(t, "ch", base_ch + ch.marks[("nfree", tg - 1)])
                for kc in range(16):
                    mi = t.matmul(ps_n[:, :], sgw_sb[:, kc:kc + 1],
                                  xn_sb[:, ((tg % 2) * 16 + kc) * 512:((tg % 2) * 16 + kc) * 512 + 512],
                                  start=(kc == 0), stop=(kc == 15))
                p.inc(mi, "mm", 1)
            ch.emit_all("t", t)

        @bk.vector
        def _(v):
            p.wait(v, "bar", b0["bar"])
            ch.emit_all("v", v)
            p.bar(v)

        @bk.scalar
        def _(sc):
            p.wait(sc, "bar", b0["bar"])
            ch.emit_all("sc", sc)
            p.bar(sc)

        @bk.gpsimd
        def _(g):
            p.wait(g, "ch", base_ch + len(ch.ops))
            p.bar(g)


def _emit_experts(p, layer, nhc, xn_ap, acc, cst, wr, xn_sb, comb_dram,
                  sg_bc, c_bc, u, nexp=10):
    nc, bk = p.nc, p.block
    b0 = p.base()
    NEXP = nexp
    halves = [(e, h) for e in range(NEXP) for h in range(2)]

    mm_gu, mm_dn, dD_gu, dD_dn = {}, {}, {}, {}
    mm_c, dD_c = 0, 0
    dC_c = 0
    for (e, half) in halves:
        for hc in range(nhc):
            dD_c += 2
            dD_gu[(e, half, hc)] = dD_c
            for tg2 in range(2):
                mm_c += 1
                mm_gu[(e, half, hc, tg2)] = mm_c
        for mc in range(16):
            dC_c += 1
            dD_dn[(e, half, mc)] = dC_c
            for tg2 in range(2):
                mm_c += 1
                mm_dn[(e, half, mc, tg2)] = mm_c

    with (
        nc.sbuf_tensor(f"x{u}_cr", [1, 2048], F32) as comb_row,
        nc.sbuf_tensor(f"x{u}_h", [128, nhc * 1024], BF16) as h_sb,
        nc.sbuf_tensor(f"x{u}_wg", [128, 2 * 2048], BF16) as wg_sb,
        nc.sbuf_tensor(f"x{u}_wu", [128, 2 * 2048], BF16) as wu_sb,
        nc.sbuf_tensor(f"x{u}_wd", [128, 2 * nhc * 128], BF16) as wd_sb,
        nc.sbuf_tensor(f"x{u}_sg", [128, 2 * 512], BF16) as sg_st,
        nc.sbuf_tensor(f"x{u}_uq", [128, 2 * 512], BF16) as uq_st,
        nc.psum_tensor(f"x{u}_pg", [128, 2 * 512], F32) as ps_g,
        nc.psum_tensor(f"x{u}_pu", [128, 2 * 512], F32) as ps_u,
        nc.psum_tensor(f"x{u}_pd", [128, 2 * 512], F32) as ps_d,
        nc.psum_tensor(f"x{u}_pc", [128, 512], F32) as ps_c,
    ):
        ch = Chain(p, p.cnt["ch"])
        base_ch = p.cnt["ch"]
        for (e, half) in halves:
            lin = e * 2 + half
            if e < 8 and half == 0:
                for j in range(4):
                    w = [("dB", b0["dB"] + 16 * (e + 1))] if j == 0 else ()
                    ch.add("t", lambda t, j=j: t.matmul(
                        ps_c[:, :], cst.ones_f[0:1, :],
                        comb_row[0:1, j * 512:j * 512 + 512],
                        start=True, stop=True), waits=w)
                    ch.add("v", lambda v, j=j: v.tensor_copy(
                        c_bc[:, j * 512:j * 512 + 512], ps_c[:, :]))
                ch.mark(("cbc", e))
            cb = sg_bc if e >= 8 else c_bc
            for hc in range(nhc):
                for tg2 in range(2):
                    w = [("mm", b0["mm"] + mm_gu[(e, half, hc, tg2)])]
                    if hc == 0 and tg2 == 0 and lin > 0:
                        pe, ph = halves[lin - 1]
                        w.append(("mm", b0["mm"] + mm_dn[(pe, ph, 15, 1)]))
                    ch.add("sc", lambda sc, s=tg2: sc.activation(
                        sg_st[:, s * 512:s * 512 + 512],
                        ps_g[:, s * 512:s * 512 + 512], AF.Silu), waits=w)
                    toff = half * 1024 + tg2 * 512
                    ch.add("v", lambda v, s=tg2, cb=cb, toff=toff: v.tensor_mul(
                        uq_st[:, s * 512:s * 512 + 512],
                        ps_u[:, s * 512:s * 512 + 512],
                        cb[:, toff:toff + 512]))
                    ch.add("v", lambda v, s=tg2, hc=hc: v.tensor_mul(
                        h_sb[:, hc * 1024 + s * 512: hc * 1024 + s * 512 + 512],
                        sg_st[:, s * 512:s * 512 + 512],
                        uq_st[:, s * 512:s * 512 + 512]))
                    ch.mark(("gu", e, half, hc, tg2))
            for mc in range(16):
                for tg2 in range(2):
                    w = [("mm", b0["mm"] + mm_dn[(e, half, mc, tg2)])]
                    toff = half * 1024 + tg2 * 512
                    s = (mc * 2 + tg2) % 2
                    ch.add("v", lambda v, mc=mc, toff=toff, s=s: v.tensor_add(
                        acc[:, mc * 2048 + toff: mc * 2048 + toff + 512],
                        acc[:, mc * 2048 + toff: mc * 2048 + toff + 512],
                        ps_d[:, s * 512:s * 512 + 512]), waits=w)
                    ch.mark(("dn", e, half, mc, tg2))

        def wslice(kind, e, idx):
            if e < 8:
                if kind in ("g", "u"):
                    ap = wr["wg" if kind == "g" else "wu"][e, idx]
                else:
                    ap = wr["wd"][e, idx]
            else:
                off = (e - 8) * nhc
                if kind in ("g", "u"):
                    ap = wr["swg" if kind == "g" else "swu"][off + idx]
                else:
                    ap = wr["swd"][idx, :, off:off + nhc, :]
            return ap.rearrange("p kc m -> p (kc m)")

        @bk.gpsimd
        def _(g):
            p.wait(g, "bar", b0["bar"])
            gi = 0
            for (e, half) in halves:
                for hc in range(nhc):
                    if gi >= 2:
                        q, phc = divmod((e * 2 + half) * nhc + hc - 2, nhc)
                        pe, ph = divmod(q, 2)
                        p.wait(g, "mm", b0["mm"] + mm_gu[(pe, ph, phc, 1)])
                    p.sdma(g, "dD", wg_sb[:, (hc % 2) * 2048:(hc % 2) * 2048 + 2048],
                           wslice("g", e, hc))
                    p.sdma(g, "dD", wu_sb[:, (hc % 2) * 2048:(hc % 2) * 2048 + 2048],
                           wslice("u", e, hc))
                    gi += 1
                for mc in range(16):
                    di = (e * 2 + half) * 16 + mc
                    if di >= 2:
                        lin2 = di - 2
                        q, pmc = divmod(lin2, 16)
                        pe, ph = divmod(q, 2)
                        p.wait(g, "mm", b0["mm"] + mm_dn[(pe, ph, pmc, 1)])
                    p.sdma(g, "dC", wd_sb[:, (mc % 2) * nhc * 128:(mc % 2 + 1) * nhc * 128],
                           wslice("d", e, mc))

        @bk.sync
        def _(s):
            p.wait(s, "bar", b0["bar"])
            for (e, half) in halves:
                lin = e * 2 + half
                if e < 8 and half == 0:
                    if e >= 1:
                        p.wait(s, "ch", base_ch + ch.marks[("cbc", e - 1)])
                    p.sdma(s, "dB", comb_row[0:1, :], comb_dram[e:e + 1, :])
                if lin >= 1:
                    pe, ph = halves[lin - 1]
                    p.wait(s, "mm", b0["mm"] + mm_gu[(pe, ph, nhc - 1, 1)])
                p.sdma(s, "dA",
                       xn_sb[:, :].rearrange("p (kc t) -> p kc t", t=1024),
                       xn_ap[:, :, half * 1024:half * 1024 + 1024]
                           .rearrange("kc p t -> p kc t"))

        @bk.tensor
        def _(t):
            p.wait(t, "bar", b0["bar"])
            for (e, half) in halves:
                lin = e * 2 + half
                if e < 8 and half == 0:
                    ch.emit_until("t", t, ch.marks[("cbc", e)])
                p.wait(t, "dA", b0["dA"] + 16 * (lin + 1))
                for hc in range(nhc):
                    p.wait(t, "dD", b0["dD"] + dD_gu[(e, half, hc)] * 16)
                    for tg2 in range(2):
                        lin2 = (e * 2 + half) * nhc * 2 + hc * 2 + tg2
                        if lin2 >= 2:
                            q2, ptg2 = divmod(lin2 - 2, 2)
                            q3, phc = divmod(q2, nhc)
                            pe, ph = divmod(q3, 2)
                            p.wait(t, "ch", base_ch +
                                   ch.marks[("gu", pe, ph, phc, ptg2)])
                        for kc in range(16):
                            t.matmul(
                                ps_g[:, tg2 * 512:tg2 * 512 + 512],
                                wg_sb[:, (hc % 2) * 2048 + kc * 128:(hc % 2) * 2048 + kc * 128 + 128],
                                xn_sb[:, kc * 1024 + tg2 * 512: kc * 1024 + tg2 * 512 + 512],
                                start=(kc == 0), stop=(kc == 15))
                        for kc in range(16):
                            mu = t.matmul(
                                ps_u[:, tg2 * 512:tg2 * 512 + 512],
                                wu_sb[:, (hc % 2) * 2048 + kc * 128:(hc % 2) * 2048 + kc * 128 + 128],
                                xn_sb[:, kc * 1024 + tg2 * 512: kc * 1024 + tg2 * 512 + 512],
                                start=(kc == 0), stop=(kc == 15))
                        p.inc(mu, "mm", 1)
                for mc in range(16):
                    p.wait(t, "dC", b0["dC"] + dD_dn[(e, half, mc)] * 16)
                    for tg2 in range(2):
                        if mc == 0 and tg2 == 0:
                            p.wait(t, "ch", base_ch +
                                   ch.marks[("gu", e, half, nhc - 1, 1)])
                        gidx = (e * 2 + half) * 32 + mc * 2 + tg2
                        if gidx >= 2:
                            lin2 = gidx - 2
                            q2, ptg2 = divmod(lin2, 2)
                            q3, pmc = divmod(q2, 16)
                            pe, ph = divmod(q3, 2)
                            p.wait(t, "ch", base_ch +
                                   ch.marks[("dn", pe, ph, pmc, ptg2)])
                        s = (mc * 2 + tg2) % 2
                        toff = half * 1024 + tg2 * 512
                        for kc in range(nhc):
                            md = t.matmul(
                                ps_d[:, s * 512:s * 512 + 512],
                                wd_sb[:, (mc % 2) * nhc * 128 + kc * 128:(mc % 2) * nhc * 128 + kc * 128 + 128],
                                h_sb[:, kc * 1024 + tg2 * 512: kc * 1024 + tg2 * 512 + 512],
                                start=(kc == 0), stop=(kc == nhc - 1))
                        p.inc(md, "mm", 1)
            ch.emit_all("t", t)

        @bk.vector
        def _(v):
            p.wait(v, "bar", b0["bar"])
            ch.emit_all("v", v)
            p.bar(v)

        @bk.scalar
        def _(sc):
            p.wait(sc, "bar", b0["bar"])
            ch.emit_all("sc", sc)
            p.bar(sc)

        @bk.gpsimd
        def _(g):
            p.wait(g, "ch", base_ch + len(ch.ops))
            p.bar(g)



def _const_dram(nc):
    return {
        "ident": nc.dram_tensor("ident", [128, 128], F32, kind="ExternalInput").ap(),
        "ones": nc.dram_tensor("ones", [128, 128], F32, kind="ExternalInput").ap(),
        "masks": nc.dram_tensor("masks", [128, 8 * 512], F32, kind="ExternalInput").ap(),
        "fconst": nc.dram_tensor("fconst", [128, 8], F32, kind="ExternalInput").ap(),
    }


def build_attn(layer):
    nc = bass.Bass("TRN2", target_bir_lowering=False)
    aps = _const_dram(nc)
    xn_all = nc.dram_tensor("xn_all", [4, 16, 128, TL], BF16,
                            kind="ExternalInput").ap()
    cs = nc.dram_tensor("cstab", [128, TSEQ], F32, kind="ExternalInput").ap()
    cs2 = nc.dram_tensor("cstab2", [128, TSEQ], F32, kind="ExternalInput").ap()
    nkv = 1 if layer == 0 else 4
    w_aps = {
        "q": nc.dram_tensor("wq_sl", [4, 128, 16, 128], F32,
                            kind="ExternalInput").ap(),
        "k": nc.dram_tensor("wk_sl", [nkv, 128, 16, 128], F32,
                            kind="ExternalInput").ap(),
        "v": nc.dram_tensor("wv_sl", [nkv, 128, 16, 128], F32,
                            kind="ExternalInput").ap(),
    }
    gam_aps = None
    if layer == 1:
        gam_aps = (nc.dram_tensor("qgam", [128, 1], F32, kind="ExternalInput").ap(),
                   nc.dram_tensor("kgam", [128, 1], F32, kind="ExternalInput").ap())
    ao = nc.dram_tensor("ao_out", [4, 4, 128, TL], BF16,
                        kind="ExternalOutput").ap()
    p = P(nc)
    cst = Consts()
    emit_consts(p, cst, aps)
    emit_attn(p, layer, xn_all, ao, cst, w_aps, cs, cs2, gam_aps)
    p.close()
    return nc


def pack_attn_weights(inputs):
    w0q = lhsT_pack(inputs["L0_attn_wq"])
    w0k = lhsT_pack(inputs["L0_attn_wk"])
    w0v = lhsT_pack(inputs["L0_attn_wv"])
    qkv = inputs["L1_attn_w_qkv"]
    w1q = lhsT_pack(qkv[:, 0:2048])
    w1k = lhsT_pack(qkv[:, 2048:4096])
    w1v = lhsT_pack(qkv[:, 4096:6144])
    return (w0q, w0k, w0v), (w1q, w1k, w1v)


def attn_in_map(layer, core, packs, consts_in, c, inputs, xn_all_for_batch):
    g = core % 4
    (w0q, w0k, w0v), (w1q, w1k, w1v) = packs
    if layer == 0:
        wq, wk, wv = w0q[4 * g:4 * g + 4], w0k[g:g + 1], w0v[g:g + 1]
        cs, cs2 = c["cs0"], c["cs0s"]
    else:
        wq, wk, wv = w1q[4 * g:4 * g + 4], w1k[4 * g:4 * g + 4], w1v[4 * g:4 * g + 4]
        cs, cs2 = c["cs1"], c["cs1s"]
    m = {"xn_all": xn_all_for_batch, "cstab": cs, "cstab2": cs2,
         "wq_sl": np.ascontiguousarray(wq), "wk_sl": np.ascontiguousarray(wk),
         "wv_sl": np.ascontiguousarray(wv), **consts_in}
    if layer == 1:
        m["qgam"] = inputs["L1_attn_q_gamma"].reshape(128, 1).astype(np.float32)
        m["kgam"] = inputs["L1_attn_k_gamma"].reshape(128, 1).astype(np.float32)
    return m


def build_pre():
    nc = bass.Bass("TRN2", target_bir_lowering=False)
    x = nc.dram_tensor("x_sl", [TL, D], F32, kind="ExternalInput").ap()
    aps = {
        "ident": nc.dram_tensor("ident", [128, 128], F32, kind="ExternalInput").ap(),
        "ones": nc.dram_tensor("ones", [128, 128], F32, kind="ExternalInput").ap(),
        "masks": nc.dram_tensor("masks", [128, 8 * 512], F32, kind="ExternalInput").ap(),
        "fconst": nc.dram_tensor("fconst", [128, 8], F32, kind="ExternalInput").ap(),
    }
    xt = nc.dram_tensor("xt_out", [16, 128, TL], BF16, kind="ExternalOutput").ap()
    xn = nc.dram_tensor("xn_out", [16, 128, TL], BF16, kind="ExternalOutput").ap()

    p = P(nc)
    cst = Consts()
    emit_consts(p, cst, aps)
    acc = p.stack.enter_context(nc.sbuf_tensor("acc", [128, 16 * TL], BF16))
    emit_load_x(p, x, acc, cst)
    emit_ln(p, acc, xn, cst)
    emit_store_acc(p, acc, xt)
    p.close()
    return nc


def _rest_w_dram(nc, layer):
    nhc = NHC0 if layer == 0 else NHC1
    return {
        "wo": nc.dram_tensor("wo", [16, 128, 16, 128], F32, kind="ExternalInput").ap(),
        "gate": nc.dram_tensor("gate", [128, 128], F32, kind="ExternalInput").ap(),
        "sgw": nc.dram_tensor("sgw", [128, 16], F32, kind="ExternalInput").ap(),
        "wg": nc.dram_tensor("wg", [8, nhc, 128, 16, 128], F32, kind="ExternalInput").ap(),
        "wu": nc.dram_tensor("wu", [8, nhc, 128, 16, 128], F32, kind="ExternalInput").ap(),
        "wd": nc.dram_tensor("wd", [8, 16, 128, nhc, 128], F32, kind="ExternalInput").ap(),
        "swg": nc.dram_tensor("swg", [2 * nhc, 128, 16, 128], F32, kind="ExternalInput").ap(),
        "swu": nc.dram_tensor("swu", [2 * nhc, 128, 16, 128], F32, kind="ExternalInput").ap(),
        "swd": nc.dram_tensor("swd", [16, 128, 2 * nhc, 128], F32, kind="ExternalInput").ap(),
    }


def pack_rest_weights(inputs, layer):
    if layer == 0:
        wo = lhsT_pack(inputs["L0_attn_wo"])
        gate = pack_small_m(np.ascontiguousarray(
            inputs["L0_ffn_gate_w"].T)).reshape(128, 128)
        sgw = pack_small_m(inputs["L0_ffn_shgate_w"].reshape(D, 1)).reshape(128, 16)
        wg = np.stack([lhsT_pack(inputs["L0_ffn_w_gate"][e]) for e in range(8)])
        wu = np.stack([lhsT_pack(inputs["L0_ffn_w_up"][e]) for e in range(8)])
        wd = np.stack([lhsT_pack(inputs["L0_ffn_w_down"][e]) for e in range(8)])
        swg = lhsT_pack(inputs["L0_ffn_shw_gate"])
        swu = lhsT_pack(inputs["L0_ffn_shw_up"])
        swd = lhsT_pack(inputs["L0_ffn_shw_down"])
    else:
        wo = lhsT_pack(inputs["L1_attn_w_out"])
        gate = pack_small_m(np.ascontiguousarray(
            inputs["L1_ffn_gate_w"].T)).reshape(128, 128)
        sgw = pack_small_m(np.ascontiguousarray(
            inputs["L1_ffn_se_gate_w"].T)).reshape(128, 16)
        wg = np.stack([lhsT_pack(np.ascontiguousarray(inputs["L1_ffn_w_gate"][e].T))
                       for e in range(8)])
        wu = np.stack([lhsT_pack(np.ascontiguousarray(inputs["L1_ffn_w_up"][e].T))
                       for e in range(8)])
        wd = np.stack([lhsT_pack(np.ascontiguousarray(inputs["L1_ffn_w_down"][e].T))
                       for e in range(8)])
        swg = lhsT_pack(np.ascontiguousarray(inputs["L1_ffn_sh_gate"].T))
        swu = lhsT_pack(np.ascontiguousarray(inputs["L1_ffn_sh_up"].T))
        swd = lhsT_pack(np.ascontiguousarray(inputs["L1_ffn_sh_down"].T))
    return {"wo": wo, "gate": gate, "sgw": sgw, "wg": wg, "wu": wu, "wd": wd,
            "swg": swg, "swu": swu, "swd": swd}


def build_rest(layer):
    nc = bass.Bass("TRN2", target_bir_lowering=False)
    aps = _const_dram(nc)
    xt_in = nc.dram_tensor("xt_in", [16, 128, TL], BF16, kind="ExternalInput").ap()
    xn_in = nc.dram_tensor("xn_in", [16, 128, TL], BF16, kind="ExternalInput").ap()
    ao_in = nc.dram_tensor("ao_in", [16, 128, TL], BF16, kind="ExternalInput").ap()
    wr = _rest_w_dram(nc, layer)
    p = P(nc)
    cst = Consts()
    emit_consts(p, cst, aps)
    acc = p.stack.enter_context(nc.sbuf_tensor("acc", [128, 16 * TL], BF16))
    emit_load_acc(p, xt_in, acc)
    emit_rest(p, layer, xn_in, ao_in, acc, cst, wr)
    if layer == 0:
        xt_out = nc.dram_tensor("xt_out", [16, 128, TL], BF16,
                                kind="ExternalOutput").ap()
        xn_out = nc.dram_tensor("xn_out", [16, 128, TL], BF16,
                                kind="ExternalOutput").ap()
        emit_ln(p, acc, xn_out, cst)
        emit_store_acc(p, acc, xt_out)
    else:
        y = nc.dram_tensor("y_sl", [TL, D], F32, kind="ExternalOutput").ap()
        emit_out(p, acc, y, cst)
    p.close()
    return nc



def _consts_inmap():
    c = make_consts()
    return {"ident": c["ident"], "ones": c["ones"], "masks": c["masks"],
            "fconst": c["fconst"]}, c


_BUILD_CACHE = {}


def _get(name, fn):
    if name not in _BUILD_CACHE:
        _BUILD_CACHE[name] = fn()
    return _BUILD_CACHE[name]


def kernel(**inputs):
    consts_in, c = _consts_inmap()
    packs = pack_attn_weights(inputs)
    x = inputs["L0_attn_x"]
    cores = list(range(NCORE))

    nc = _get("pre", build_pre)
    im = []
    for core in cores:
        b, g = core // 4, core % 4
        im.append({"x_sl": np.ascontiguousarray(x[b, g * TL:(g + 1) * TL]),
                   **consts_in})
    r = run_bass_kernel_spmd(nc, im, core_ids=cores)
    xt = [r.results[core]["xt_out"] for core in cores]
    xn = [r.results[core]["xn_out"] for core in cores]

    for layer in (0, 1):
        xn_all = [np.stack([xn[b * 4 + g] for g in range(4)]) for b in range(B)]
        nc = _get(f"attn{layer}", lambda: build_attn(layer))
        im = [attn_in_map(layer, core, packs, consts_in, c, inputs,
                          xn_all[core // 4]) for core in cores]
        r = run_bass_kernel_spmd(nc, im, core_ids=cores)
        ao = [r.results[core]["ao_out"] for core in cores]
        ao_sl = []
        for core in cores:
            b, g = core // 4, core % 4
            ao_sl.append(np.ascontiguousarray(
                np.concatenate([ao[b * 4 + src][g] for src in range(4)], axis=0)))

        wrn = pack_rest_weights(inputs, layer)
        nc = _get(f"rest{layer}", lambda: build_rest(layer))
        im = []
        for core in cores:
            m = {"xt_in": xt[core], "xn_in": xn[core], "ao_in": ao_sl[core],
                 **{k: wrn[k] for k in ("wo", "gate", "sgw", "wg", "wu", "wd",
                                        "swg", "swu", "swd")},
                 **consts_in}
            im.append(m)
        r = run_bass_kernel_spmd(nc, im, core_ids=cores)
        if layer == 0:
            xt = [r.results[core]["xt_out"] for core in cores]
            xn = [r.results[core]["xn_out"] for core in cores]
        else:
            y = np.empty((B, TSEQ, D), dtype=np.float32)
            for core in cores:
                b, g = core // 4, core % 4
                y[b, g * TL:(g + 1) * TL] = r.results[core]["y_sl"]
            return y


def _emit_r1_only(p, layer, xn_ap, ao_ap, acc, cst, wr, stage):
    nc, bk = p.nc, p.block
    nhc = NHC0 if layer == 0 else NHC1
    u = p.uid()
    b0 = p.base()
    with (
        nc.sbuf_tensor(f"r{u}_ao", [128, 16 * 2048], BF16) as ao_sb,
        nc.sbuf_tensor(f"r{u}_wo", [128, 2 * 2048], BF16) as wo_sb,
        nc.psum_tensor(f"r{u}_pw", [128, 2 * 512], F32) as pw,
    ):
        @bk.sync
        def _(s):
            p.wait(s, "bar", b0["bar"])
            p.inc(s.dma_start(
                out=ao_sb[:, :].rearrange("p (h t) -> p h t", t=2048),
                in_=ao_ap.rearrange("h p t -> p h t")), "dA", 16)

        @bk.gpsimd
        def _(g):
            p.wait(g, "bar", b0["bar"])
            for mc in range(16):
                if mc >= 2:
                    p.wait(g, "mm", b0["mm"] + (mc - 1) * 4)
                p.sdma(g, "dC", wo_sb[:, (mc % 2) * 2048:(mc % 2) * 2048 + 2048],
                       wr["wo"][mc].rearrange("p kc m -> p (kc m)"))

        @bk.tensor
        def _(t):
            p.wait(t, "bar", b0["bar"])
            p.wait(t, "dA", b0["dA"] + 16)
            for mc in range(16):
                p.wait(t, "dC", b0["dC"] + 16 * (mc + 1))
                for tg in range(4):
                    idx = mc * 4 + tg
                    if idx >= 2:
                        p.wait(t, "ev", b0["ev"] + idx - 1)
                    for hc in range(16):
                        mi = t.matmul(
                            pw[:, (idx % 2) * 512:(idx % 2) * 512 + 512],
                            wo_sb[:, (mc % 2) * 2048 + hc * 128:(mc % 2) * 2048 + hc * 128 + 128],
                            ao_sb[:, hc * 2048 + tg * 512: hc * 2048 + tg * 512 + 512],
                            start=(hc == 0), stop=(hc == 15))
                    p.inc(mi, "mm", 1)

        @bk.vector
        def _(v):
            p.wait(v, "bar", b0["bar"])
            for mc in range(16):
                for tg in range(4):
                    idx = mc * 4 + tg
                    p.wait(v, "mm", b0["mm"] + idx + 1)
                    a = acc[:, mc * 2048 + tg * 512: mc * 2048 + tg * 512 + 512]
                    p.inc(v.tensor_add(a, a, pw[:, (idx % 2) * 512:(idx % 2) * 512 + 512]),
                          "ev", 1)
            p.bar(v)

    if stage == "r2":
        comb_dram = nc.dram_tensor(f"r{u}_combd", [8, 2048], F32).ap()
        with (
            nc.sbuf_tensor(f"r{u}_xn", [128, 16 * 1024], BF16) as xn_sb,
            nc.sbuf_tensor(f"r{u}_comb", [8, 2048], F32) as comb_fm,
            nc.sbuf_tensor(f"r{u}_sgbc", [128, 2048], F32) as sg_bc,
        ):
            _emit_router(p, layer, xn_ap, cst, wr, xn_sb, comb_fm,
                         comb_dram, sg_bc, u)
